# version 08
# baseline (speedup 1.0000x reference)
import numpy as np
import concourse.bass as bass
import concourse.mybir as mybir
from concourse.bass_utils import run_bass_kernel_spmd

F32 = mybir.dt.float32
F16 = mybir.dt.float16
ALU = mybir.AluOpType
AX = mybir.AxisListType
ACT = mybir.ActivationFunctionType

C = 32
DCH = 16
OG = 4
D = 2048
EPS = 1e-5
M32 = float(1.5 * 2.0 ** 23)
M64 = float(1.5 * 2.0 ** 52)
LAG = 9
SR = 16

SUMU, SUMSQ, NM, MSQE, VAREP, RSTD, UMAX, MX, XS, INV, A2, B2P = range(12)
WMAX, WS, WINV, UMIN = 12, 13, 14, 15

RACE_DETECT = False


class Track:
    def __init__(self, name, sem, step):
        self.name, self.sem, self.step, self.n = name, sem, step, 0

    def sig(self, reg, inst, name=None):
        inst.then_inc(self.sem, self.step)
        self.n += self.step
        if name is not None:
            reg[name] = (self.name, self.n)


def _mk_wait(reg, trks):
    def mk(consumer_eng):
        waited = {}

        def wait(name):
            ent = reg.get(name)
            if ent is None:
                return
            tname, val = ent
            if waited.get(tname, 0) >= val:
                return
            consumer_eng.wait_ge(trks[tname].sem, val)
            waited[tname] = val
        return wait
    return mk


def build(reg_in):
    reg = {}
    nc = bass.Bass("TRN2", target_bir_lowering=False,
                   detect_race_conditions=RACE_DETECT)

    u_h = nc.dram_tensor("u", [C * 128, D], F32, kind="ExternalInput").ap()
    A_h = nc.dram_tensor("A", [8, 2, 2], F32, kind="ExternalInput").ap()
    wbr_h = nc.dram_tensor("Wbr", [D, 64], F32, kind="ExternalInput").ap()
    wbs_h = nc.dram_tensor("Wbs", [64, 256], F32, kind="ExternalInput").ap()
    wcr_h = nc.dram_tensor("Wcr", [256, 64], F32, kind="ExternalInput").ap()
    wcm_h = nc.dram_tensor("Wcm", [64, D], F32, kind="ExternalInput").ap()
    dv_h = nc.dram_tensor("Dv", [D], F32, kind="ExternalInput").ap()
    wff_h = nc.dram_tensor("Wff", [D, D], F32, kind="ExternalInput").ap()
    bias_h = nc.dram_tensor("bias", [D], F32, kind="ExternalInput").ap()
    out_h = nc.dram_tensor("out", [C * 128, D], F32, kind="ExternalOutput").ap()
    wdh = nc.dram_tensor("wdh", [D, D], F16, kind="Internal").ap()
    vh = nc.dram_tensor("vh", [C * 128, D], F16, kind="Internal").ap()

    dv2 = dv_h.rearrange("(a b) -> a b", a=1)
    bias2 = bias_h.rearrange("(a b) -> a b", a=1)
    a2d = A_h.rearrange("a b c -> (a b c)").rearrange("(a b) -> a b", a=1)

    from contextlib import ExitStack
    with ExitStack() as es:
        sb = lambda n, s, dt=F32: es.enter_context(nc.sbuf_tensor(n, s, dt))
        wT = sb("wT", [128, DCH, D], F16)
        u16 = sb("u16", [128, 2, D], F16)
        v16 = sb("v16", [128, 2, D], F16)
        vr16 = sb("vr16", [128, 2, D], F16)
        y16 = sb("y16", [128, 2, D], F16)
        s16 = sb("s16", [128, 2, D], F16)
        o32 = sb("o32", [128, 2, D], F32)
        drep = sb("drep", [128, D], F16)
        brep = sb("brep", [128, D], F16)
        wbr2 = sb("wbr2", [128, DCH, 64], F16)
        wcm16 = sb("wcm16", [64, D], F16)
        z2sb = sb("z2sb", [64, 512], F16)
        stats = sb("stats", [128, SR, 16], F32)
        mwk = sb("mwk", [128, 4], F32)
        exn = sb("exn", [128, 1], F32)
        slab = [sb("slabA", [128, DCH, 512], F16),
                sb("slabB", [128, DCH, 512], F16)]
        pes = ExitStack()
        pb = lambda n, s_, dt=F32: pes.enter_context(nc.sbuf_tensor(n, s_, dt))
        ones32 = pb("ones32", [1, 128], F32)
        ones16 = pb("ones16", [1, 128], F16)
        ident = pb("ident", [128, 128], F32)
        abuf = pb("abuf", [1, 32], F32)
        arep = pb("arep", [64, 32], F32)
        wbsA = pb("wbsA", [64, 256], F32)
        wbsB = pb("wbsB", [64, 256], F32)
        wbsT = pb("wbsT", [128, 2, 64], F32)
        wcrS = pb("wcrS", [128, 2, 64], F32)
        wsm = pb("wsm", [64, 64], F32)
        wbrS = pb("wbrS", [128, DCH, 64], F32)
        wbrT = pb("wbrT", [64, 2, 128], F32)
        bftmp = pb("bftmp", [64, 128], F32)
        pes.close()

        z3ps = es.enter_context(nc.psum_tensor("z3ps", [128, D], F32))
        ffnps = es.enter_context(nc.psum_tensor("ffnps", [128, D], F32))

        sem = lambda n: es.enter_context(nc.semaphore(n))
        SC = Track("SC", sem("SC"), 1)
        DV = Track("DV", sem("DV"), 1)
        PE = Track("PE", sem("PE"), 1)
        GP = Track("GP", sem("GP"), 1)
        STAGE = Track("STAGE", sem("STAGE"), 16)
        AB = Track("AB", sem("AB"), 16)
        GST = Track("GST", sem("GST"), 16)
        UL = [Track(f"UL{i}", sem(f"UL{i}"), 16) for i in range(2)]
        UR = [Track(f"UR{i}", sem(f"UR{i}"), 16) for i in range(2)]
        VR = [Track(f"VR{i}", sem(f"VR{i}"), 16) for i in range(2)]
        W = [Track(f"W{i}", sem(f"W{i}"), 16) for i in range(2)]
        WH = [Track(f"WH{i}", sem(f"WH{i}"), 16) for i in range(2)]
        VH = [Track(f"VH{i}", sem(f"VH{i}"), 16) for i in range(2)]
        TRQ = Track("TRQ", sem("TRQ"), 16)
        STO = [Track(f"STO{i}", sem(f"STO{i}"), 16) for i in range(2)]
        trks = {t.name: t for t in (SC, DV, PE, GP, STAGE, AB, GST, TRQ,
                                    *UL, *UR, *VR, *W, *WH, *VH, *STO)}
        mkw = _mk_wait(reg_in, trks)
        block = es.enter_context(nc.Block())

        @block.sync
        def _(sync):
            w = mkw(sync)

            def wgrp(g):
                for oc in range(4 * g, 4 * g + 4):
                    w(f"wh:{oc}")
                for dch in range(DCH):
                    TRQ.sig(reg, sync.dma_start_transpose(
                        out=wT[:, dch, g * 512:(g + 1) * 512],
                        in_=wdh[g * 512:(g + 1) * 512,
                                dch * 128:(dch + 1) * 128]),
                        f"tr:wg:{g}" if dch == DCH - 1 else None)
                sync.wait_ge(TRQ.sem, TRQ.n)

            def vgrp(b):
                for cc in range(4 * b, 4 * b + 4):
                    w(f"vh:{cc}")
                if b >= 2:
                    w(f"pe:ffnB:{4 * b - 5}")
                for dch in range(DCH):
                    TRQ.sig(reg, sync.dma_start_transpose(
                        out=slab[b % 2][:, dch, :],
                        in_=vh[4 * b * 128:(4 * b + 4) * 128,
                               dch * 128:(dch + 1) * 128]),
                        f"tv:b:{b}" if dch == DCH - 1 else None)
                sync.wait_ge(TRQ.sem, TRQ.n)

            for g in range(4):
                wgrp(g)
            for b in range(C // 4):
                vgrp(b)

        @block.scalar
        def _(scalar):
            w = mkw(scalar)
            scalar.preload_act_table(ACT.Silu)
            AB.sig(reg, scalar.dma_start(out=abuf[:], in_=a2d), "ld:abuf")
            for dst, src in ((wbsA[:], wbs_h),
                             (wcrS[:, 0, :], wcr_h[0:128, :]),
                             (wcrS[:, 1, :], wcr_h[128:256, :])):
                STAGE.sig(reg, scalar.dma_start(out=dst, in_=src))
            for t in range(DCH):
                STAGE.sig(reg, scalar.dma_start(
                    out=wbrS[:, t, :], in_=wbr_h[t * 128:(t + 1) * 128, :]))
            reg["ld:stage"] = ("STAGE", STAGE.n)
            for oc in range(2):
                W[oc % 2].sig(reg, scalar.dma_start(
                    out=o32[:, oc % 2, :], in_=wff_h[oc * 128:(oc + 1) * 128, :]),
                    f"ld:w:{oc}")

            def quant(oc):
                w(f"dv:wsinv:{oc}")
                w(f"ld:w:{oc}")
                if oc >= 2:
                    w(f"wh:{oc - 2}")
                SC.sig(reg, scalar.activation(
                    out=y16[:, oc % 2, :], in_=o32[:, oc % 2, :],
                    func=ACT.Identity, scale=stats[:, oc % 4, WINV:WINV + 1]),
                    f"sc:vw:{oc}")
                if oc + 2 < DCH:
                    scalar.wait_ge(SC.sem, SC.n)
                    w(f"dv:wmax:{oc}")
                    W[oc % 2].sig(reg, scalar.dma_start(
                        out=o32[:, (oc + 2) % 2, :],
                        in_=wff_h[(oc + 2) * 128:(oc + 3) * 128, :]),
                        f"ld:w:{oc + 2}")

            def urload(cc):
                if cc >= 2:
                    w(f"st:{cc - 2}")
                else:
                    w(f"sc:vw:{14 + cc}")
                    w(f"dv:wmax:{14 + cc}")
                UR[cc % 2].sig(reg, scalar.dma_start(
                    out=o32[:, cc % 2, :], in_=u_h[cc * 128:(cc + 1) * 128, :]),
                    f"ur:{cc}")

            def vrload(cc):
                w(f"vh:{cc}")
                if cc >= 2:
                    w(f"dv:svd:{cc - 2}")
                VR[cc % 2].sig(reg, scalar.dma_start(
                    out=vr16[:, cc % 2, :], in_=vh[cc * 128:(cc + 1) * 128, :]),
                    f"vr:{cc}")

            def front_a(cc):
                st = lambda k: stats[:, cc % SR, k:k + 1]
                w(f"ld:u16:{cc}")
                if cc >= 2:
                    w(f"vh:{cc - 2}")
                if cc >= SR:
                    w(f"dv:sg:{cc - SR}")
                scalar.wait_ge(SC.sem, SC.n)
                SC.sig(reg, scalar.activation(
                    out=v16[:, cc % 2, :], in_=u16[:, cc % 2, :],
                    func=ACT.Identity, accum_out=st(SUMU)), f"sc:id:{cc}")
                scalar.wait_ge(SC.sem, SC.n)
                SC.sig(reg, scalar.activation(
                    out=v16[:, cc % 2, :], in_=u16[:, cc % 2, :],
                    func=ACT.Square, accum_out=st(SUMSQ)), f"sc:sq:{cc}")

            def front_b(cc):
                st = lambda k: stats[:, cc % SR, k:k + 1]
                w(f"dv:stats2:{cc}")
                scalar.wait_ge(SC.sem, SC.n)
                SC.sig(reg, scalar.activation(
                    out=v16[:, cc % 2, :], in_=u16[:, cc % 2, :],
                    func=ACT.Identity, scale=st(A2), bias=st(B2P)), f"sc:v:{cc}")

            def z2e(b):
                w(f"pe:z2b:{b}")
                if b >= 1:
                    w(f"pe:z3:{4 * b - 1}")
                SC.sig(reg, scalar.activation(
                    out=z2sb[:], in_=z3ps[0:64, 0:512], func=ACT.Identity),
                    f"sc:z2e:{b}")

            def silu(cc):
                w(f"dv:yB:{cc}")
                SC.sig(reg, scalar.activation(
                    out=y16[:, cc % 2, :], in_=y16[:, cc % 2, :], func=ACT.Silu),
                    f"sc:silu:{cc}")

            evsteps = [(["pe:arep"], arep[:], z3ps[0:64, 0:32], "sc:arepe"),
                       (["pe:wsmT:0"], wbsT[:, 0, :], ffnps[0:128, 0:64],
                        "sc:wsmTe:0"),
                       (["pe:wsmT:1"], wbsT[:, 1, :], ffnps[0:128, 0:64],
                        "sc:wsmTe:1"),
                       (["pe:wsm"], wsm[:], z3ps[0:64, 128:192], "sc:wsme")]
            for t in range(DCH):
                evsteps.append(([f"pe:wbrT:{t}"], wbrT[:, t % 2, :],
                                ffnps[0:64, 128:256], f"sc:wbrTe:{t}"))
                evsteps.append(([f"pe:wbr2:{t}"], wbr2[:, t, :],
                                z3ps[0:128, 256:320], f"sc:wbr2e:{t}"))
            evsteps.append((["pe:drep"], drep[:], ffnps[:, 0:D], "sc:drepe"))
            evsteps.append((["pe:brep"], brep[:], z3ps[:, 0:D], "sc:brepe"))

            def drain_ev(n):
                for _ in range(n):
                    if not evsteps:
                        return
                    waits, dst, src, nm = evsteps.pop(0)
                    for x in waits:
                        w(x)
                    SC.sig(reg, scalar.activation(out=dst, in_=src,
                                                  func=ACT.Identity), nm)

            drain_ev(1)
            for oc in range(DCH):
                quant(oc)
                if oc >= 2:
                    drain_ev(3)
            drain_ev(99)
            urload(0)
            urload(1)
            nf = min(LAG, C)
            front_a(0)
            for k in range(1, nf):
                front_a(k)
                front_b(k - 1)
            vrload(0)
            vrload(1)
            for c in range(C):
                if c >= 1:
                    silu(c - 1)
                if c + LAG < C:
                    front_a(c + LAG)
                kb = min(LAG, C) - 1 + c
                if kb < C:
                    front_b(kb)
                if c == 0:
                    z2e(0)
                if c % 4 == 3 and c + 1 < C:
                    z2e((c + 1) // 4)
                if c >= 1 and c + 1 < C:
                    urload(c + 1)
                if c + 2 < C:
                    vrload(c + 2)
            silu(C - 1)

        @block.vector
        def _(vector):
            w = mkw(vector)
            selfw = lambda: vector.wait_ge(DV.sem, DV.n)

            DV.sig(reg, vector.memset(mwk[:, 0:1], 1.0))
            selfw()
            DV.sig(reg, vector.tensor_scalar(out=mwk[:, 0:1], in0=mwk[:, 0:1],
                                             scalar1=M64, scalar2=-M64,
                                             op0=ALU.add, op1=ALU.add))
            selfw()
            DV.sig(reg, vector.tensor_scalar(out=mwk[:, 1:2], in0=mwk[:, 0:1],
                                             scalar1=-M32, scalar2=M32,
                                             op0=ALU.mult, op1=ALU.add))
            selfw()
            DV.sig(reg, vector.scalar_tensor_tensor(
                out=mwk[:, 2:3], in0=mwk[:, 0:1], scalar=M64,
                in1=mwk[:, 1:2], op0=ALU.mult, op1=ALU.add), "dv:mw")
            DV.sig(reg, vector.memset(mwk[:, 3:4], 127.0))
            selfw()
            DV.sig(reg, vector.reciprocal(out=mwk[:, 3:4], in_=mwk[:, 3:4]),
                   "dv:c127")
            DV.sig(reg, vector.memset(exn[:], -0.5), "dv:exn")
            DV.sig(reg, vector.memset(ones32[:], 1.0), "dv:ones32")
            DV.sig(reg, vector.memset(ones16[:], 1.0), "dv:ones16")
            w("gp:iota")
            DV.sig(reg, vector.tensor_single_scalar(out=ident[:], in_=ident[:],
                                                    scalar=0.0, op=ALU.is_equal),
                   "dv:ident")

            def dfront(cc):
                st = lambda k: stats[:, cc % SR, k:k + 1]
                w(f"ld:u16:{cc}")
                DV.sig(reg, vector.reduce_max(out=st(UMAX), in_=u16[:, cc % 2, :],
                                              axis=AX.X), f"dv:umax:{cc}")
                DV.sig(reg, vector.tensor_reduce(out=st(UMIN),
                                                 in_=u16[:, cc % 2, :],
                                                 axis=AX.X, op=ALU.min),
                       f"dv:umin:{cc}")
                w(f"sc:sq:{cc}")
                selfw()
                DV.sig(reg, vector.tensor_scalar(out=st(NM), in0=st(SUMU),
                                                 scalar1=-1.0 / 2048.0,
                                                 scalar2=None, op0=ALU.mult))
                selfw()
                DV.sig(reg, vector.tensor_scalar(out=st(MSQE), in0=st(NM),
                                                 scalar1=st(NM), scalar2=EPS,
                                                 op0=ALU.mult, op1=ALU.subtract))
                selfw()
                DV.sig(reg, vector.scalar_tensor_tensor(
                    out=st(VAREP), in0=st(SUMSQ), scalar=1.0 / 2048.0,
                    in1=st(MSQE), op0=ALU.mult, op1=ALU.subtract),
                    f"dv:stats1:{cc}")
                selfw()
                DV.sig(reg, vector.tensor_scalar(out=st(MX), in0=st(UMAX),
                                                 scalar1=st(NM), scalar2=None,
                                                 op0=ALU.add))
                selfw()
                DV.sig(reg, vector.tensor_scalar(out=st(UMIN), in0=st(UMIN),
                                                 scalar1=st(NM), scalar2=-1.0,
                                                 op0=ALU.add, op1=ALU.mult))
                selfw()
                DV.sig(reg, vector.tensor_scalar(out=st(MX), in0=st(MX),
                                                 scalar1=st(UMIN), scalar2=None,
                                                 op0=ALU.max))
                w(f"gp:rstd:{cc}")
                selfw()
                w("dv:c127")
                DV.sig(reg, vector.tensor_scalar(out=st(XS), in0=st(MX),
                                                 scalar1=st(RSTD),
                                                 scalar2=mwk[:, 3:4],
                                                 op0=ALU.mult, op1=ALU.mult))
                selfw()
                DV.sig(reg, vector.tensor_scalar(out=st(XS), in0=st(XS),
                                                 scalar1=1e-12, scalar2=None,
                                                 op0=ALU.max))
                selfw()
                DV.sig(reg, vector.reciprocal(out=st(INV), in_=st(XS)))
                selfw()
                DV.sig(reg, vector.tensor_scalar(out=st(A2), in0=st(RSTD),
                                                 scalar1=st(INV), scalar2=None,
                                                 op0=ALU.mult))
                DV.sig(reg, vector.tensor_scalar(out=st(B2P), in0=st(NM),
                                                 scalar1=st(RSTD), scalar2=st(INV),
                                                 op0=ALU.mult, op1=ALU.mult),
                       f"dv:stats2:{cc}")

            for oc in range(DCH):
                so = lambda k: stats[:, oc % 4, k:k + 1]
                w(f"ld:w:{oc}")
                DV.sig(reg, vector.reduce_max(out=so(WMAX), in_=o32[:, oc % 2, :],
                                              axis=AX.X, apply_absolute_value=True),
                       f"dv:wmax:{oc}")
                selfw()
                w("dv:c127")
                DV.sig(reg, vector.tensor_scalar(out=so(WS), in0=so(WMAX),
                                                 scalar1=1e-8, scalar2=mwk[:, 3:4],
                                                 op0=ALU.max, op1=ALU.mult))
                selfw()
                DV.sig(reg, vector.reciprocal(out=so(WINV), in_=so(WS)),
                       f"dv:wsinv:{oc}")
                w(f"sc:vw:{oc}")
                w("dv:mw")
                DV.sig(reg, vector.tensor_scalar(out=y16[:, oc % 2, :],
                                                 in0=y16[:, oc % 2, :],
                                                 scalar1=mwk[:, 2:3],
                                                 scalar2=mwk[:, 2:3],
                                                 op0=ALU.add, op1=ALU.subtract),
                       f"dv:wq:{oc}")
                selfw()
                DV.sig(reg, vector.tensor_scalar(out=y16[:, oc % 2, :],
                                                 in0=y16[:, oc % 2, :],
                                                 scalar1=so(WS), scalar2=None,
                                                 op0=ALU.mult), f"dv:wdeq:{oc}")
                selfw()
            for cc in range(min(LAG, C)):
                dfront(cc)

            def roundb(b):
                w(f"pe:z2b:{b}")
                w("dv:mw")
                sl = slab[b % 2].ap().rearrange("p a b -> p (a b)")
                DV.sig(reg, vector.tensor_scalar(
                    out=sl, in0=sl, scalar1=mwk[:, 2:3], scalar2=mwk[:, 2:3],
                    op0=ALU.add, op1=ALU.subtract), f"dv:xqTb:{b}")

            roundb(0)

            def sgout(cc):
                w(f"sc:silu:{cc}")
                selfw()
                DV.sig(reg, vector.scalar_tensor_tensor(
                    out=y16[:, cc % 2, :],
                    in0=s16[:, cc % 2, :],
                    scalar=stats[:, cc % SR, XS:XS + 1],
                    in1=y16[:, cc % 2, :],
                    op0=ALU.mult, op1=ALU.mult), f"dv:sg:{cc}")
                w(f"ur:{cc}")
                selfw()
                DV.sig(reg, vector.tensor_add(out=o32[:, cc % 2, :],
                                              in0=y16[:, cc % 2, :],
                                              in1=o32[:, cc % 2, :]),
                       f"dv:out:{cc}")

            for c in range(C):
                st = lambda k: stats[:, c % SR, k:k + 1]
                w(f"vr:{c}")
                w("sc:drepe")
                selfw()
                DV.sig(reg, vector.tensor_mul(out=s16[:, c % 2, :],
                                              in0=vr16[:, c % 2, :], in1=drep[:]),
                       f"dv:svd:{c}")
                w(f"pe:z3:{c}")
                selfw()
                DV.sig(reg, vector.tensor_add(out=s16[:, c % 2, :],
                                              in0=z3ps[:, 0:D],
                                              in1=s16[:, c % 2, :]),
                       f"dv:s:{c}")
                if c >= 1:
                    sgout(c - 1)
                if c + LAG < C:
                    dfront(c + LAG)
                if c % 4 == 3 and c + 1 < C:
                    roundb((c + 1) // 4)
                for h in range(2):
                    w(f"pe:ffn{'AB'[h]}:{c}")
                    if h == 0:
                        if c <= 1:
                            w(f"wh:{14 + c}")
                        w("sc:brepe")
                    DV.sig(reg, vector.scalar_tensor_tensor(
                        out=y16[:, c % 2, h * 1024:(h + 1) * 1024],
                        in0=ffnps[:, h * 1024:(h + 1) * 1024], scalar=st(XS),
                        in1=brep[:, h * 1024:(h + 1) * 1024],
                        op0=ALU.mult, op1=ALU.add), f"dv:y{'AB'[h]}:{c}")
            sgout(C - 1)

        @block.tensor
        def _(tensor):
            w = mkw(tensor)
            w("ld:abuf")
            w("dv:ones32")
            PE.sig(reg, tensor.matmul(z3ps[0:64, 0:32], ones32[:, 0:64], abuf[:],
                                      start=True, stop=True), "pe:arep")
            w("gp:bfly")
            w("dv:ident")
            for j in range(2):
                if j == 1:
                    w("sc:wsmTe:0")
                PE.sig(reg, tensor.transpose(ffnps[0:128, 0:64],
                                             wbsA[:, j * 128:(j + 1) * 128],
                                             ident[0:64, 0:64]), f"pe:wsmT:{j}")
            w("sc:wsmTe:1")
            w("sc:arepe")
            for j in range(2):
                PE.sig(reg, tensor.matmul(z3ps[0:64, 128:192], wbsT[:, j, :],
                                          wcrS[:, j, :], start=(j == 0),
                                          stop=(j == 1)),
                       "pe:wsm" if j == 1 else None)
            w("sc:wsme")
            for t in range(DCH):
                if t >= 1:
                    w(f"sc:wbrTe:{t - 1}")
                PE.sig(reg, tensor.transpose(ffnps[0:64, 128:256],
                                             wbrS[:, t, :], ident[:, :]),
                       f"pe:wbrT:{t}")
                w(f"sc:wbrTe:{t}")
                if t >= 1:
                    w(f"sc:wbr2e:{t - 1}")
                PE.sig(reg, tensor.matmul(z3ps[0:128, 256:320], wbrT[:, t % 2, :],
                                          wsm[:], start=True, stop=True),
                       f"pe:wbr2:{t}")
            w("gld:all")
            w("dv:ones16")
            w("sc:wbrTe:15")
            for og in range(OG):
                PE.sig(reg, tensor.matmul(
                    ffnps[:, og * 512:(og + 1) * 512], ones16[:],
                    drep[0:1, og * 512:(og + 1) * 512], start=True, stop=True),
                    "pe:drep" if og == OG - 1 else None)
            w("sc:wbr2e:15")
            for og in range(OG):
                PE.sig(reg, tensor.matmul(
                    z3ps[:, og * 512:(og + 1) * 512], ones16[:],
                    brep[0:1, og * 512:(og + 1) * 512], start=True, stop=True),
                    "pe:brep" if og == OG - 1 else None)

            def z2burst(b):
                w(f"tv:b:{b}")
                if b >= 1:
                    w(f"dv:s:{4 * b - 1}")
                else:
                    w("sc:brepe")
                w("sc:wbr2e:15")
                for dch in range(DCH):
                    PE.sig(reg, tensor.matmul(
                        z3ps[0:64, 0:512], wbr2[:, dch, :], slab[b % 2][:, dch, :],
                        start=(dch == 0), stop=(dch == DCH - 1)),
                        f"pe:z2b:{b}" if dch == DCH - 1 else None)

            z2burst(0)
            for c in range(C):
                w(f"sc:z2e:{c // 4}")
                w("gld:all")
                if c >= 1:
                    w(f"dv:s:{c - 1}")
                for og in range(OG):
                    PE.sig(reg, tensor.matmul(
                        z3ps[:, og * 512:(og + 1) * 512],
                        z2sb[:, (c % 4) * 128:(c % 4 + 1) * 128],
                        wcm16[:, og * 512:(og + 1) * 512],
                        start=True, stop=True),
                        f"pe:z3:{c}" if og == OG - 1 else None)
                for h in range(2):
                    w(f"dv:xqTb:{c // 4}")
                    if c >= 1:
                        w(f"dv:y{'AB'[h]}:{c - 1}")
                    elif h == 0:
                        w("sc:drepe")
                    w(f"tr:wg:{2 * h}")
                    w(f"tr:wg:{2 * h + 1}")
                    for dch in range(DCH):
                        for og in (2 * h, 2 * h + 1):
                            last = dch == DCH - 1 and og == 2 * h + 1
                            PE.sig(reg, tensor.matmul(
                                ffnps[:, og * 512:(og + 1) * 512],
                                slab[(c // 4) % 2][:, dch,
                                                   (c % 4) * 128:(c % 4 + 1) * 128],
                                wT[:, dch, og * 512:(og + 1) * 512],
                                start=(dch == 0), stop=(dch == DCH - 1)),
                                f"pe:ffn{'AB'[h]}:{c}" if last else None)
                    if h == 0 and c % 4 == 3 and c + 1 < C:
                        z2burst((c + 1) // 4)

        @block.gpsimd
        def _(gpsimd):
            w = mkw(gpsimd)
            GP.sig(reg, gpsimd.iota(ident[:], pattern=[[1, 128]], base=0,
                                    channel_multiplier=-1,
                                    allow_small_or_imprecise_dtypes=True),
                   "gp:iota")
            GST.sig(reg, gpsimd.dma_start(out=wcm16[:], in_=wcm_h))
            GST.sig(reg, gpsimd.dma_start(out=drep[0:1, :], in_=dv2))
            GST.sig(reg, gpsimd.dma_start(out=brep[0:1, :], in_=bias2))
            reg["gld:all"] = ("GST", GST.n)

            def uload(cc):
                if cc >= 2:
                    w(f"sc:v:{cc - 2}")
                    w(f"dv:umin:{cc - 2}")
                UL[cc % 2].sig(reg, gpsimd.dma_start(
                    out=u16[:, cc % 2, :], in_=u_h[cc * 128:(cc + 1) * 128, :]),
                    f"ld:u16:{cc}")

            def rstd(cc):
                st = lambda k: stats[:, cc % SR, k:k + 1]
                w(f"dv:stats1:{cc}")
                w("dv:exn")
                GP.sig(reg, gpsimd.tensor_tensor(out=st(RSTD), in0=st(VAREP),
                                                 in1=exn[:], op=ALU.pow),
                       f"gp:rstd:{cc}")

            def vstore(cc):
                w(f"sc:v:{cc}")
                VH[cc % 2].sig(reg, gpsimd.dma_start(
                    out=vh[cc * 128:(cc + 1) * 128, :], in_=v16[:, cc % 2, :]),
                    f"vh:{cc}")

            def outst(cc):
                w(f"dv:out:{cc}")
                STO[cc % 2].sig(reg, gpsimd.dma_start(
                    out=out_h[cc * 128:(cc + 1) * 128, :],
                    in_=o32[:, cc % 2, :]), f"st:{cc}")

            uload(0)
            uload(1)
            gw = lambda: gpsimd.wait_ge(GP.sem, GP.n)
            w("sc:arepe")
            w("ld:stage")
            cur, nxt = wbsA, wbsB
            for lvl in range(8):
                s = 2 ** (7 - lvl)
                vsrc = cur.ap().rearrange("p (nb two s) -> p nb two s",
                                          two=2, s=s)
                vdst = nxt.ap().rearrange("p (nb two s) -> p nb two s",
                                          two=2, s=s)
                x0, x1 = vsrc[:, :, 0, :], vsrc[:, :, 1, :]
                bfv = bftmp.ap().rearrange("p (nb s) -> p nb s", s=s)
                for j in range(2):
                    a0 = arep[:, (4 * lvl + j):(4 * lvl + j + 1)]
                    a1 = arep[:, (4 * lvl + 2 + j):(4 * lvl + 3 + j)]
                    o = vdst[:, :, j, :]
                    GP.sig(reg, gpsimd.tensor_scalar(
                        out=o, in0=x1, scalar1=a1, scalar2=None,
                        op0=ALU.mult))
                    GP.sig(reg, gpsimd.tensor_scalar(
                        out=bfv, in0=x0, scalar1=a0, scalar2=None,
                        op0=ALU.mult))
                    gw()
                    GP.sig(reg, gpsimd.tensor_tensor(
                        out=o, in0=bfv, in1=o, op=ALU.add))
                    gw()
                gw()
                cur, nxt = nxt, cur
            GP.sig(reg, gpsimd.engine_nop(), "gp:bfly")
            for oc in range(DCH):
                w(f"dv:wdeq:{oc}")
                WH[oc % 2].sig(reg, gpsimd.dma_start(
                    out=wdh[oc * 128:(oc + 1) * 128, :], in_=y16[:, oc % 2, :]),
                    f"wh:{oc}")
            for k in range(min(LAG + 1, C)):
                rstd(k)
                if k + 2 < min(LAG + 1, C):
                    uload(k + 2)
                if k < min(LAG, C) - 1:
                    vstore(k)
            for c in range(C):
                if c >= 1:
                    outst(c - 1)
                if c >= 1 and c + LAG < C:
                    rstd(c + LAG)
                if c + LAG + 1 < C:
                    uload(c + LAG + 1)
                kb = min(LAG, C) - 1 + c
                if kb < C:
                    vstore(kb)
            outst(C - 1)
            gpsimd.wait_ge(STO[0].sem, STO[0].n)
            gpsimd.wait_ge(STO[1].sem, STO[1].n)

    return nc, reg


_CACHED = None


def _get_nc():
    global _CACHED
    if _CACHED is None:
        _, reg = build({})
        nc, _ = build(reg)
        _CACHED = nc
    return _CACHED


def kernel(L0_attn_u, L0_attn_A_factors, L0_attn_W_b_rank, L0_attn_W_b_state,
           L0_attn_W_c_rank, L0_attn_W_c_model, L0_attn_D_vec, L0_ffn_weight,
           L0_ffn_bias):
    nc = _get_nc()
    f = lambda a: np.ascontiguousarray(np.asarray(a), dtype=np.float32)
    base = {
        "A": f(L0_attn_A_factors), "Wbr": f(L0_attn_W_b_rank),
        "Wbs": f(L0_attn_W_b_state), "Wcr": f(L0_attn_W_c_rank),
        "Wcm": f(L0_attn_W_c_model), "Dv": f(L0_attn_D_vec),
        "Wff": f(L0_ffn_weight), "bias": f(L0_ffn_bias),
    }
    u = f(L0_attn_u)
    in_maps = [dict(base, u=np.ascontiguousarray(u[i])) for i in range(8)]
    res = run_bass_kernel_spmd(nc, in_maps, core_ids=list(range(8)))
    return np.stack([np.asarray(res.results[i]["out"], dtype=np.float32)
                     for i in range(8)], axis=0)


# version 09
# speedup vs baseline: 1.0782x; 1.0782x over previous
import numpy as np
import concourse.bass as bass
import concourse.mybir as mybir
from concourse.bass_utils import run_bass_kernel_spmd

F32 = mybir.dt.float32
F16 = mybir.dt.float16
ALU = mybir.AluOpType
AX = mybir.AxisListType
ACT = mybir.ActivationFunctionType

C = 32
DCH = 16
OG = 4
D = 2048
EPS = 1e-5
M32 = float(1.5 * 2.0 ** 23)
M64 = float(1.5 * 2.0 ** 52)
LAG = 9
SR = 16

SUMU, SUMSQ, NM, MSQE, VAREP, RSTD, UMAX, MX, XS, INV, A2, B2P = range(12)
WMAX, WS, WINV, UMIN = 12, 13, 14, 15

RACE_DETECT = False


class Track:
    def __init__(self, name, sem, step):
        self.name, self.sem, self.step, self.n = name, sem, step, 0

    def sig(self, reg, inst, name=None):
        inst.then_inc(self.sem, self.step)
        self.n += self.step
        if name is not None:
            reg[name] = (self.name, self.n)


def _mk_wait(reg, trks):
    def mk(consumer_eng):
        waited = {}

        def wait(name):
            ent = reg.get(name)
            if ent is None:
                return
            tname, val = ent
            if waited.get(tname, 0) >= val:
                return
            consumer_eng.wait_ge(trks[tname].sem, val)
            waited[tname] = val
        return wait
    return mk


def build(reg_in):
    reg = {}
    nc = bass.Bass("TRN2", target_bir_lowering=False,
                   detect_race_conditions=RACE_DETECT)

    u_h = nc.dram_tensor("u", [C * 128, D], F32, kind="ExternalInput").ap()
    A_h = nc.dram_tensor("A", [8, 2, 2], F32, kind="ExternalInput").ap()
    wbr_h = nc.dram_tensor("Wbr", [D, 64], F32, kind="ExternalInput").ap()
    wbs_h = nc.dram_tensor("Wbs", [64, 256], F32, kind="ExternalInput").ap()
    wcr_h = nc.dram_tensor("Wcr", [256, 64], F32, kind="ExternalInput").ap()
    wcm_h = nc.dram_tensor("Wcm", [64, D], F32, kind="ExternalInput").ap()
    dv_h = nc.dram_tensor("Dv", [D], F32, kind="ExternalInput").ap()
    wff_h = nc.dram_tensor("Wff", [D, D], F32, kind="ExternalInput").ap()
    bias_h = nc.dram_tensor("bias", [D], F32, kind="ExternalInput").ap()
    out_h = nc.dram_tensor("out", [C * 128, D], F32, kind="ExternalOutput").ap()
    wdh = nc.dram_tensor("wdh", [D, D], F16, kind="Internal").ap()
    vh = nc.dram_tensor("vh", [C * 128, D], F16, kind="Internal").ap()

    dv2 = dv_h.rearrange("(a b) -> a b", a=1)
    bias2 = bias_h.rearrange("(a b) -> a b", a=1)
    a2d = A_h.rearrange("a b c -> (a b c)").rearrange("(a b) -> a b", a=1)

    from contextlib import ExitStack
    with ExitStack() as es:
        sb = lambda n, s, dt=F32: es.enter_context(nc.sbuf_tensor(n, s, dt))
        wT = sb("wT", [128, DCH, D], F16)
        u16 = sb("u16", [128, 2, D], F16)
        v16 = sb("v16", [128, 2, D], F16)
        vr16 = sb("vr16", [128, 2, D], F16)
        y16 = sb("y16", [128, 2, D], F16)
        s16 = sb("s16", [128, 2, D], F16)
        o32 = sb("o32", [128, 2, D], F32)
        drep = sb("drep", [128, D], F16)
        brep = sb("brep", [128, D], F16)
        wbr2 = sb("wbr2", [128, DCH, 64], F16)
        wcm16 = sb("wcm16", [64, D], F16)
        z2sb = sb("z2sb", [64, 512], F16)
        stats = sb("stats", [128, SR, 16], F32)
        mwk = sb("mwk", [128, 4], F32)
        exn = sb("exn", [128, 1], F32)
        slab = [sb("slabA", [128, DCH, 512], F16),
                sb("slabB", [128, DCH, 512], F16)]
        pes = ExitStack()
        pb = lambda n, s_, dt=F32: pes.enter_context(nc.sbuf_tensor(n, s_, dt))
        ones32 = pb("ones32", [1, 128], F32)
        ones16 = pb("ones16", [1, 128], F16)
        ident = pb("ident", [128, 128], F32)
        abuf = pb("abuf", [1, 32], F32)
        arep = pb("arep", [64, 32], F32)
        wbsA = pb("wbsA", [64, 256], F32)
        wbsB = pb("wbsB", [64, 256], F32)
        wbsT = pb("wbsT", [128, 2, 64], F32)
        wcrS = pb("wcrS", [128, 2, 64], F32)
        wsm = pb("wsm", [64, 64], F32)
        wbrS = pb("wbrS", [128, DCH, 64], F32)
        wbrT = pb("wbrT", [64, 2, 128], F32)
        bftmp = pb("bftmp", [64, 128], F32)
        pes.close()

        z3ps = es.enter_context(nc.psum_tensor("z3ps", [128, D], F32))
        ffnps = es.enter_context(nc.psum_tensor("ffnps", [128, D], F32))

        sem = lambda n: es.enter_context(nc.semaphore(n))
        SC = Track("SC", sem("SC"), 1)
        DV = Track("DV", sem("DV"), 1)
        PE = Track("PE", sem("PE"), 1)
        GP = Track("GP", sem("GP"), 1)
        STAGE = Track("STAGE", sem("STAGE"), 16)
        AB = Track("AB", sem("AB"), 16)
        GST = Track("GST", sem("GST"), 16)
        UL = [Track(f"UL{i}", sem(f"UL{i}"), 16) for i in range(2)]
        UR = [Track(f"UR{i}", sem(f"UR{i}"), 16) for i in range(2)]
        VR = [Track(f"VR{i}", sem(f"VR{i}"), 16) for i in range(2)]
        W = [Track(f"W{i}", sem(f"W{i}"), 16) for i in range(2)]
        WH = [Track(f"WH{i}", sem(f"WH{i}"), 16) for i in range(2)]
        VH = [Track(f"VH{i}", sem(f"VH{i}"), 16) for i in range(2)]
        TRQ = Track("TRQ", sem("TRQ"), 16)
        STO = [Track(f"STO{i}", sem(f"STO{i}"), 16) for i in range(2)]
        trks = {t.name: t for t in (SC, DV, PE, GP, STAGE, AB, GST, TRQ,
                                    *UL, *UR, *VR, *W, *WH, *VH, *STO)}
        mkw = _mk_wait(reg_in, trks)
        block = es.enter_context(nc.Block())

        @block.sync
        def _(sync):
            w = mkw(sync)

            def wgrp(g):
                for oc in range(4 * g, 4 * g + 4):
                    w(f"wh:{oc}")
                for dch in range(DCH):
                    TRQ.sig(reg, sync.dma_start_transpose(
                        out=wT[:, dch, g * 512:(g + 1) * 512],
                        in_=wdh[g * 512:(g + 1) * 512,
                                dch * 128:(dch + 1) * 128]),
                        f"tr:wg:{g}" if dch == DCH - 1 else None)
                sync.wait_ge(TRQ.sem, TRQ.n)

            def vgrp(b):
                for cc in range(4 * b, 4 * b + 4):
                    w(f"vh:{cc}")
                if b >= 2:
                    w(f"pe:ffnB:{4 * b - 5}")
                for dch in range(DCH):
                    TRQ.sig(reg, sync.dma_start_transpose(
                        out=slab[b % 2][:, dch, :],
                        in_=vh[4 * b * 128:(4 * b + 4) * 128,
                               dch * 128:(dch + 1) * 128]),
                        f"tv:b:{b}" if dch == DCH - 1 else None)
                sync.wait_ge(TRQ.sem, TRQ.n)

            for g in range(4):
                wgrp(g)
            for b in range(C // 4):
                vgrp(b)

        @block.scalar
        def _(scalar):
            w = mkw(scalar)
            scalar.preload_act_table(ACT.Silu)
            AB.sig(reg, scalar.dma_start(out=abuf[:], in_=a2d), "ld:abuf")
            for dst, src in ((wbsA[:], wbs_h),
                             (wcrS[:, 0, :], wcr_h[0:128, :]),
                             (wcrS[:, 1, :], wcr_h[128:256, :])):
                STAGE.sig(reg, scalar.dma_start(out=dst, in_=src))
            for t in range(DCH):
                STAGE.sig(reg, scalar.dma_start(
                    out=wbrS[:, t, :], in_=wbr_h[t * 128:(t + 1) * 128, :]))
            reg["ld:stage"] = ("STAGE", STAGE.n)
            for oc in range(2):
                W[oc % 2].sig(reg, scalar.dma_start(
                    out=o32[:, oc % 2, :], in_=wff_h[oc * 128:(oc + 1) * 128, :]),
                    f"ld:w:{oc}")

            def quant(oc):
                if oc + 2 < DCH:
                    w(f"dv:vws:{oc}")
                    W[oc % 2].sig(reg, scalar.dma_start(
                        out=o32[:, (oc + 2) % 2, :],
                        in_=wff_h[(oc + 2) * 128:(oc + 3) * 128, :]),
                        f"ld:w:{oc + 2}")

            def urload(cc):
                if cc >= 2:
                    w(f"st:{cc - 2}")
                else:
                    w(f"dv:vws:{14 + cc}")
                UR[cc % 2].sig(reg, scalar.dma_start(
                    out=o32[:, cc % 2, :], in_=u_h[cc * 128:(cc + 1) * 128, :]),
                    f"ur:{cc}")

            def vrload(cc):
                w(f"vh:{cc}")
                if cc >= 2:
                    w(f"dv:svd:{cc - 2}")
                VR[cc % 2].sig(reg, scalar.dma_start(
                    out=vr16[:, cc % 2, :], in_=vh[cc * 128:(cc + 1) * 128, :]),
                    f"vr:{cc}")

            def front_a(cc):
                st = lambda k: stats[:, cc % SR, k:k + 1]
                w(f"ld:u16:{cc}")
                if cc >= 2:
                    w(f"vh:{cc - 2}")
                if cc >= SR:
                    w(f"dv:sg:{cc - SR}")
                scalar.wait_ge(SC.sem, SC.n)
                SC.sig(reg, scalar.activation(
                    out=v16[:, cc % 2, :], in_=u16[:, cc % 2, :],
                    func=ACT.Identity, accum_out=st(SUMU)), f"sc:id:{cc}")
                scalar.wait_ge(SC.sem, SC.n)
                SC.sig(reg, scalar.activation(
                    out=v16[:, cc % 2, :], in_=u16[:, cc % 2, :],
                    func=ACT.Square, accum_out=st(SUMSQ)), f"sc:sq:{cc}")

            def front_b(cc):
                st = lambda k: stats[:, cc % SR, k:k + 1]
                w(f"dv:stats2:{cc}")
                scalar.wait_ge(SC.sem, SC.n)
                SC.sig(reg, scalar.activation(
                    out=v16[:, cc % 2, :], in_=u16[:, cc % 2, :],
                    func=ACT.Identity, scale=st(A2), bias=st(B2P)), f"sc:v:{cc}")

            def z2e(b):
                w(f"pe:z2b:{b}")
                if b >= 1:
                    w(f"pe:z3:{4 * b - 1}")
                SC.sig(reg, scalar.activation(
                    out=z2sb[:], in_=z3ps[0:64, 0:512], func=ACT.Identity),
                    f"sc:z2e:{b}")

            def silu(cc):
                w(f"dv:yB:{cc}")
                SC.sig(reg, scalar.activation(
                    out=y16[:, cc % 2, :], in_=y16[:, cc % 2, :], func=ACT.Silu),
                    f"sc:silu:{cc}")

            evsteps = [(["pe:arep"], arep[:], z3ps[0:64, 0:32], "sc:arepe"),
                       (["pe:wsmT:0"], wbsT[:, 0, :], ffnps[0:128, 0:64],
                        "sc:wsmTe:0"),
                       (["pe:wsmT:1"], wbsT[:, 1, :], ffnps[0:128, 0:64],
                        "sc:wsmTe:1"),
                       (["pe:wsm"], wsm[:], z3ps[0:64, 128:192], "sc:wsme")]
            for t in range(DCH):
                evsteps.append(([f"pe:wbrT:{t}"], wbrT[:, t % 2, :],
                                ffnps[0:64, 128:256], f"sc:wbrTe:{t}"))
                evsteps.append(([f"pe:wbr2:{t}"], wbr2[:, t, :],
                                z3ps[0:128, 256:320], f"sc:wbr2e:{t}"))
            evsteps.append((["pe:drep"], drep[:], ffnps[:, 0:D], "sc:drepe"))
            evsteps.append((["pe:brep"], brep[:], z3ps[:, 0:D], "sc:brepe"))

            def drain_ev(n):
                for _ in range(n):
                    if not evsteps:
                        return
                    waits, dst, src, nm = evsteps.pop(0)
                    for x in waits:
                        w(x)
                    SC.sig(reg, scalar.activation(out=dst, in_=src,
                                                  func=ACT.Identity), nm)

            drain_ev(1)
            for oc in range(DCH):
                quant(oc)
                if oc >= 2:
                    drain_ev(3)
            drain_ev(99)
            urload(0)
            urload(1)
            nf = min(LAG, C)
            front_a(0)
            for k in range(1, nf):
                front_a(k)
                front_b(k - 1)
            vrload(0)
            vrload(1)
            for c in range(C):
                if c >= 1:
                    silu(c - 1)
                if c + LAG < C:
                    front_a(c + LAG)
                kb = min(LAG, C) - 1 + c
                if kb < C:
                    front_b(kb)
                if c == 0:
                    z2e(0)
                if c % 4 == 3 and c + 1 < C:
                    z2e((c + 1) // 4)
                if c >= 1 and c + 1 < C:
                    urload(c + 1)
                if c + 2 < C:
                    vrload(c + 2)
            silu(C - 1)

        @block.vector
        def _(vector):
            w = mkw(vector)
            selfw = lambda: vector.wait_ge(DV.sem, DV.n)

            DV.sig(reg, vector.memset(mwk[:, 0:1], 1.0))
            selfw()
            DV.sig(reg, vector.tensor_scalar(out=mwk[:, 0:1], in0=mwk[:, 0:1],
                                             scalar1=M64, scalar2=-M64,
                                             op0=ALU.add, op1=ALU.add))
            selfw()
            DV.sig(reg, vector.tensor_scalar(out=mwk[:, 1:2], in0=mwk[:, 0:1],
                                             scalar1=-M32, scalar2=M32,
                                             op0=ALU.mult, op1=ALU.add))
            selfw()
            DV.sig(reg, vector.scalar_tensor_tensor(
                out=mwk[:, 2:3], in0=mwk[:, 0:1], scalar=M64,
                in1=mwk[:, 1:2], op0=ALU.mult, op1=ALU.add), "dv:mw")
            DV.sig(reg, vector.memset(mwk[:, 3:4], 127.0))
            selfw()
            DV.sig(reg, vector.reciprocal(out=mwk[:, 3:4], in_=mwk[:, 3:4]),
                   "dv:c127")
            DV.sig(reg, vector.memset(exn[:], -0.5), "dv:exn")
            DV.sig(reg, vector.memset(ones32[:], 1.0), "dv:ones32")
            DV.sig(reg, vector.memset(ones16[:], 1.0), "dv:ones16")
            w("gp:iota")
            DV.sig(reg, vector.tensor_single_scalar(out=ident[:], in_=ident[:],
                                                    scalar=0.0, op=ALU.is_equal),
                   "dv:ident")

            def dfront(cc):
                st = lambda k: stats[:, cc % SR, k:k + 1]
                w(f"ld:u16:{cc}")
                DV.sig(reg, vector.reduce_max(out=st(UMAX), in_=u16[:, cc % 2, :],
                                              axis=AX.X), f"dv:umax:{cc}")
                DV.sig(reg, vector.tensor_reduce(out=st(UMIN),
                                                 in_=u16[:, cc % 2, :],
                                                 axis=AX.X, op=ALU.min),
                       f"dv:umin:{cc}")
                w(f"sc:sq:{cc}")
                selfw()
                DV.sig(reg, vector.tensor_scalar(out=st(NM), in0=st(SUMU),
                                                 scalar1=-1.0 / 2048.0,
                                                 scalar2=None, op0=ALU.mult))
                selfw()
                DV.sig(reg, vector.tensor_scalar(out=st(MSQE), in0=st(NM),
                                                 scalar1=st(NM), scalar2=EPS,
                                                 op0=ALU.mult, op1=ALU.subtract))
                selfw()
                DV.sig(reg, vector.scalar_tensor_tensor(
                    out=st(VAREP), in0=st(SUMSQ), scalar=1.0 / 2048.0,
                    in1=st(MSQE), op0=ALU.mult, op1=ALU.subtract),
                    f"dv:stats1:{cc}")
                selfw()
                DV.sig(reg, vector.tensor_scalar(out=st(MX), in0=st(UMAX),
                                                 scalar1=st(NM), scalar2=None,
                                                 op0=ALU.add))
                selfw()
                DV.sig(reg, vector.tensor_scalar(out=st(UMIN), in0=st(UMIN),
                                                 scalar1=st(NM), scalar2=-1.0,
                                                 op0=ALU.add, op1=ALU.mult))
                selfw()
                DV.sig(reg, vector.tensor_scalar(out=st(MX), in0=st(MX),
                                                 scalar1=st(UMIN), scalar2=None,
                                                 op0=ALU.max))
                w(f"gp:rstd:{cc}")
                selfw()
                w("dv:c127")
                DV.sig(reg, vector.tensor_scalar(out=st(XS), in0=st(MX),
                                                 scalar1=st(RSTD),
                                                 scalar2=mwk[:, 3:4],
                                                 op0=ALU.mult, op1=ALU.mult))
                selfw()
                DV.sig(reg, vector.tensor_scalar(out=st(XS), in0=st(XS),
                                                 scalar1=1e-12, scalar2=None,
                                                 op0=ALU.max))
                selfw()
                DV.sig(reg, vector.reciprocal(out=st(INV), in_=st(XS)))
                selfw()
                DV.sig(reg, vector.tensor_scalar(out=st(A2), in0=st(RSTD),
                                                 scalar1=st(INV), scalar2=None,
                                                 op0=ALU.mult))
                DV.sig(reg, vector.tensor_scalar(out=st(B2P), in0=st(NM),
                                                 scalar1=st(RSTD), scalar2=st(INV),
                                                 op0=ALU.mult, op1=ALU.mult),
                       f"dv:stats2:{cc}")

            for oc in range(DCH):
                so = lambda k: stats[:, oc % 4, k:k + 1]
                w(f"ld:w:{oc}")
                DV.sig(reg, vector.reduce_max(out=so(WMAX), in_=o32[:, oc % 2, :],
                                              axis=AX.X, apply_absolute_value=True),
                       f"dv:wmax:{oc}")
                selfw()
                w("dv:c127")
                DV.sig(reg, vector.tensor_scalar(out=so(WS), in0=so(WMAX),
                                                 scalar1=1e-8, scalar2=mwk[:, 3:4],
                                                 op0=ALU.max, op1=ALU.mult))
                selfw()
                DV.sig(reg, vector.reciprocal(out=so(WINV), in_=so(WS)))
                selfw()
                if oc >= 2:
                    w(f"wh:{oc - 2}")
                DV.sig(reg, vector.tensor_scalar(out=y16[:, oc % 2, :],
                                                 in0=o32[:, oc % 2, :],
                                                 scalar1=so(WINV), scalar2=None,
                                                 op0=ALU.mult), f"dv:vws:{oc}")
                selfw()
                w("dv:mw")
                DV.sig(reg, vector.tensor_scalar(out=y16[:, oc % 2, :],
                                                 in0=y16[:, oc % 2, :],
                                                 scalar1=mwk[:, 2:3],
                                                 scalar2=mwk[:, 2:3],
                                                 op0=ALU.add, op1=ALU.subtract),
                       f"dv:wq:{oc}")
                selfw()
                DV.sig(reg, vector.tensor_scalar(out=y16[:, oc % 2, :],
                                                 in0=y16[:, oc % 2, :],
                                                 scalar1=so(WS), scalar2=None,
                                                 op0=ALU.mult), f"dv:wdeq:{oc}")
                selfw()
                if oc == 1:
                    w("sc:arepe")
                    w("ld:stage")
                    cur, nxt = wbsA, wbsB
                    for lvl in range(8):
                        s = 2 ** (7 - lvl)
                        vsrc = cur.ap().rearrange("p (nb two s) -> p nb two s",
                                                  two=2, s=s)
                        vdst = nxt.ap().rearrange("p (nb two s) -> p nb two s",
                                                  two=2, s=s)
                        x0, x1 = vsrc[:, :, 0, :], vsrc[:, :, 1, :]
                        for j in range(2):
                            a0 = arep[:, (4 * lvl + j):(4 * lvl + j + 1)]
                            a1 = arep[:, (4 * lvl + 2 + j):(4 * lvl + 3 + j)]
                            o = vdst[:, :, j, :]
                            DV.sig(reg, vector.tensor_scalar(
                                out=o, in0=x1, scalar1=a1, scalar2=None,
                                op0=ALU.mult))
                            selfw()
                            DV.sig(reg, vector.scalar_tensor_tensor(
                                out=o, in0=x0, scalar=a0, in1=o,
                                op0=ALU.mult, op1=ALU.add))
                        selfw()
                        cur, nxt = nxt, cur
                    DV.sig(reg, vector.engine_nop(), "dv:bfly")
            for cc in range(min(LAG, C)):
                dfront(cc)

            def roundb(b):
                w(f"pe:z2b:{b}")
                w("dv:mw")
                sl = slab[b % 2].ap().rearrange("p a b -> p (a b)")
                DV.sig(reg, vector.tensor_scalar(
                    out=sl, in0=sl, scalar1=mwk[:, 2:3], scalar2=mwk[:, 2:3],
                    op0=ALU.add, op1=ALU.subtract), f"dv:xqTb:{b}")

            roundb(0)

            def sgout(cc):
                w(f"sc:silu:{cc}")
                selfw()
                DV.sig(reg, vector.scalar_tensor_tensor(
                    out=y16[:, cc % 2, :],
                    in0=s16[:, cc % 2, :],
                    scalar=stats[:, cc % SR, XS:XS + 1],
                    in1=y16[:, cc % 2, :],
                    op0=ALU.mult, op1=ALU.mult), f"dv:sg:{cc}")
                w(f"ur:{cc}")
                selfw()
                DV.sig(reg, vector.tensor_add(out=o32[:, cc % 2, :],
                                              in0=y16[:, cc % 2, :],
                                              in1=o32[:, cc % 2, :]),
                       f"dv:out:{cc}")

            for c in range(C):
                st = lambda k: stats[:, c % SR, k:k + 1]
                w(f"vr:{c}")
                w("sc:drepe")
                selfw()
                DV.sig(reg, vector.tensor_mul(out=s16[:, c % 2, :],
                                              in0=vr16[:, c % 2, :], in1=drep[:]),
                       f"dv:svd:{c}")
                w(f"pe:z3:{c}")
                selfw()
                DV.sig(reg, vector.tensor_add(out=s16[:, c % 2, :],
                                              in0=z3ps[:, 0:D],
                                              in1=s16[:, c % 2, :]),
                       f"dv:s:{c}")
                if c >= 1:
                    sgout(c - 1)
                if c + LAG < C:
                    dfront(c + LAG)
                if c % 4 == 3 and c + 1 < C:
                    roundb((c + 1) // 4)
                for h in range(2):
                    w(f"pe:ffn{'AB'[h]}:{c}")
                    if h == 0:
                        if c <= 1:
                            w(f"wh:{14 + c}")
                        w("sc:brepe")
                    DV.sig(reg, vector.scalar_tensor_tensor(
                        out=y16[:, c % 2, h * 1024:(h + 1) * 1024],
                        in0=ffnps[:, h * 1024:(h + 1) * 1024], scalar=st(XS),
                        in1=brep[:, h * 1024:(h + 1) * 1024],
                        op0=ALU.mult, op1=ALU.add), f"dv:y{'AB'[h]}:{c}")
            sgout(C - 1)

        @block.tensor
        def _(tensor):
            w = mkw(tensor)
            w("ld:abuf")
            w("dv:ones32")
            PE.sig(reg, tensor.matmul(z3ps[0:64, 0:32], ones32[:, 0:64], abuf[:],
                                      start=True, stop=True), "pe:arep")
            w("dv:bfly")
            w("dv:ident")
            for j in range(2):
                if j == 1:
                    w("sc:wsmTe:0")
                PE.sig(reg, tensor.transpose(ffnps[0:128, 0:64],
                                             wbsA[:, j * 128:(j + 1) * 128],
                                             ident[0:64, 0:64]), f"pe:wsmT:{j}")
            w("sc:wsmTe:1")
            w("sc:arepe")
            for j in range(2):
                PE.sig(reg, tensor.matmul(z3ps[0:64, 128:192], wbsT[:, j, :],
                                          wcrS[:, j, :], start=(j == 0),
                                          stop=(j == 1)),
                       "pe:wsm" if j == 1 else None)
            w("sc:wsme")
            for t in range(DCH):
                if t >= 1:
                    w(f"sc:wbrTe:{t - 1}")
                PE.sig(reg, tensor.transpose(ffnps[0:64, 128:256],
                                             wbrS[:, t, :], ident[:, :]),
                       f"pe:wbrT:{t}")
                w(f"sc:wbrTe:{t}")
                if t >= 1:
                    w(f"sc:wbr2e:{t - 1}")
                PE.sig(reg, tensor.matmul(z3ps[0:128, 256:320], wbrT[:, t % 2, :],
                                          wsm[:], start=True, stop=True),
                       f"pe:wbr2:{t}")
            w("gld:all")
            w("dv:ones16")
            w("sc:wbrTe:15")
            for og in range(OG):
                PE.sig(reg, tensor.matmul(
                    ffnps[:, og * 512:(og + 1) * 512], ones16[:],
                    drep[0:1, og * 512:(og + 1) * 512], start=True, stop=True),
                    "pe:drep" if og == OG - 1 else None)
            w("sc:wbr2e:15")
            for og in range(OG):
                PE.sig(reg, tensor.matmul(
                    z3ps[:, og * 512:(og + 1) * 512], ones16[:],
                    brep[0:1, og * 512:(og + 1) * 512], start=True, stop=True),
                    "pe:brep" if og == OG - 1 else None)

            def z2burst(b):
                w(f"tv:b:{b}")
                if b >= 1:
                    w(f"dv:s:{4 * b - 1}")
                else:
                    w("sc:brepe")
                w("sc:wbr2e:15")
                for dch in range(DCH):
                    PE.sig(reg, tensor.matmul(
                        z3ps[0:64, 0:512], wbr2[:, dch, :], slab[b % 2][:, dch, :],
                        start=(dch == 0), stop=(dch == DCH - 1)),
                        f"pe:z2b:{b}" if dch == DCH - 1 else None)

            z2burst(0)
            for c in range(C):
                w(f"sc:z2e:{c // 4}")
                w("gld:all")
                if c >= 1:
                    w(f"dv:s:{c - 1}")
                for og in range(OG):
                    PE.sig(reg, tensor.matmul(
                        z3ps[:, og * 512:(og + 1) * 512],
                        z2sb[:, (c % 4) * 128:(c % 4 + 1) * 128],
                        wcm16[:, og * 512:(og + 1) * 512],
                        start=True, stop=True),
                        f"pe:z3:{c}" if og == OG - 1 else None)
                for h in range(2):
                    w(f"dv:xqTb:{c // 4}")
                    if c >= 1:
                        w(f"dv:y{'AB'[h]}:{c - 1}")
                    elif h == 0:
                        w("sc:drepe")
                    w(f"tr:wg:{2 * h}")
                    w(f"tr:wg:{2 * h + 1}")
                    for dch in range(DCH):
                        for og in (2 * h, 2 * h + 1):
                            last = dch == DCH - 1 and og == 2 * h + 1
                            PE.sig(reg, tensor.matmul(
                                ffnps[:, og * 512:(og + 1) * 512],
                                slab[(c // 4) % 2][:, dch,
                                                   (c % 4) * 128:(c % 4 + 1) * 128],
                                wT[:, dch, og * 512:(og + 1) * 512],
                                start=(dch == 0), stop=(dch == DCH - 1)),
                                f"pe:ffn{'AB'[h]}:{c}" if last else None)
                    if h == 0 and c % 4 == 3 and c + 1 < C:
                        z2burst((c + 1) // 4)

        @block.gpsimd
        def _(gpsimd):
            w = mkw(gpsimd)
            GP.sig(reg, gpsimd.iota(ident[:], pattern=[[1, 128]], base=0,
                                    channel_multiplier=-1,
                                    allow_small_or_imprecise_dtypes=True),
                   "gp:iota")
            GST.sig(reg, gpsimd.dma_start(out=wcm16[:], in_=wcm_h))
            GST.sig(reg, gpsimd.dma_start(out=drep[0:1, :], in_=dv2))
            GST.sig(reg, gpsimd.dma_start(out=brep[0:1, :], in_=bias2))
            reg["gld:all"] = ("GST", GST.n)

            def uload(cc):
                if cc >= 2:
                    w(f"sc:v:{cc - 2}")
                    w(f"dv:umin:{cc - 2}")
                UL[cc % 2].sig(reg, gpsimd.dma_start(
                    out=u16[:, cc % 2, :], in_=u_h[cc * 128:(cc + 1) * 128, :]),
                    f"ld:u16:{cc}")

            def rstd(cc):
                st = lambda k: stats[:, cc % SR, k:k + 1]
                w(f"dv:stats1:{cc}")
                w("dv:exn")
                GP.sig(reg, gpsimd.tensor_tensor(out=st(RSTD), in0=st(VAREP),
                                                 in1=exn[:], op=ALU.pow),
                       f"gp:rstd:{cc}")

            def vstore(cc):
                w(f"sc:v:{cc}")
                VH[cc % 2].sig(reg, gpsimd.dma_start(
                    out=vh[cc * 128:(cc + 1) * 128, :], in_=v16[:, cc % 2, :]),
                    f"vh:{cc}")

            def outst(cc):
                w(f"dv:out:{cc}")
                STO[cc % 2].sig(reg, gpsimd.dma_start(
                    out=out_h[cc * 128:(cc + 1) * 128, :],
                    in_=o32[:, cc % 2, :]), f"st:{cc}")

            uload(0)
            uload(1)
            for oc in range(DCH):
                w(f"dv:wdeq:{oc}")
                WH[oc % 2].sig(reg, gpsimd.dma_start(
                    out=wdh[oc * 128:(oc + 1) * 128, :], in_=y16[:, oc % 2, :]),
                    f"wh:{oc}")
            for k in range(min(LAG + 1, C)):
                rstd(k)
                if k + 2 < min(LAG + 1, C):
                    uload(k + 2)
                if k < min(LAG, C) - 1:
                    vstore(k)
            for c in range(C):
                if c >= 1:
                    outst(c - 1)
                if c >= 1 and c + LAG < C:
                    rstd(c + LAG)
                if c + LAG + 1 < C:
                    uload(c + LAG + 1)
                kb = min(LAG, C) - 1 + c
                if kb < C:
                    vstore(kb)
            outst(C - 1)
            gpsimd.wait_ge(STO[0].sem, STO[0].n)
            gpsimd.wait_ge(STO[1].sem, STO[1].n)

    return nc, reg


_CACHED = None


def _get_nc():
    global _CACHED
    if _CACHED is None:
        _, reg = build({})
        nc, _ = build(reg)
        _CACHED = nc
    return _CACHED


def kernel(L0_attn_u, L0_attn_A_factors, L0_attn_W_b_rank, L0_attn_W_b_state,
           L0_attn_W_c_rank, L0_attn_W_c_model, L0_attn_D_vec, L0_ffn_weight,
           L0_ffn_bias):
    nc = _get_nc()
    f = lambda a: np.ascontiguousarray(np.asarray(a), dtype=np.float32)
    base = {
        "A": f(L0_attn_A_factors), "Wbr": f(L0_attn_W_b_rank),
        "Wbs": f(L0_attn_W_b_state), "Wcr": f(L0_attn_W_c_rank),
        "Wcm": f(L0_attn_W_c_model), "Dv": f(L0_attn_D_vec),
        "Wff": f(L0_ffn_weight), "bias": f(L0_ffn_bias),
    }
    u = f(L0_attn_u)
    in_maps = [dict(base, u=np.ascontiguousarray(u[i])) for i in range(8)]
    res = run_bass_kernel_spmd(nc, in_maps, core_ids=list(range(8)))
    return np.stack([np.asarray(res.results[i]["out"], dtype=np.float32)
                     for i in range(8)], axis=0)
